# version 01
# baseline (speedup 1.0000x reference)
import contextlib
import numpy as np
import ml_dtypes
import concourse.bass as bass
import concourse.mybir as mybir
from concourse.bass_utils import run_bass_kernel_spmd

F32 = mybir.dt.float32
BF16 = mybir.dt.bfloat16
ALU = mybir.AluOpType
ACTF = mybir.ActivationFunctionType
AX = mybir.AxisListType

S, D, DH, NH = 4096, 2048, 128, 8
DC = D // 128
NT, NG = S // 128, S // 512
HID, HT = 5632, 44
TOK = 2048
NQT = NT
SCALE = 1.0 / float(np.sqrt(128.0))
MNEG = -1.0e10
BF = ml_dtypes.bfloat16

DEBUG = False
DEBUG_DUMPS = True
STAGE = 4
P2_HEADS = NH
P2_QT = NQT


class Sem:
    __slots__ = ("h", "n", "owner")

    def __init__(self, h, owner=None):
        self.h = h
        self.n = 0
        self.owner = owner


class Prog:
    ENGS = ("sync", "gpsimd", "vector", "scalar", "tensor")

    def __init__(self):
        self.ops = {e: [] for e in self.ENGS}
        self.pending = {e: [] for e in self.ENGS}
        self.dummy = {}
        self.dummy2 = None

    def add(self, eng, fn, waits=(), inc=None, k=1, nopinc=False):
        ws = self.pending[eng] + [(s, v) for (s, v) in waits if s is not None and v > 0]
        self.pending[eng] = []
        if inc is not None:
            inc.n += k
        self.ops[eng].append((fn, ws, inc, k, nopinc, inc.n if inc else 0))
        return inc.n if inc else None

    def fence(self, sems):
        snap = [(s, s.n) for s in sems if s.n > 0]
        for e in self.ENGS:
            self.pending[e] = self.pending[e] + list(snap)

    def emit(self, block):
        for eng in self.ENGS:
            ops = self.ops[eng]

            def mk(ops=ops, eng=eng):
                def emitter(e):
                    seen = {}
                    last_dummy = None
                    for fn, ws, inc, k, nopinc, incval in ops:
                        for (s, v) in ws:
                            if seen.get(id(s), -1) >= v:
                                continue
                            seen[id(s)] = v
                            e.wait_ge(s.h, v)
                        r = fn(e)
                        if inc is not None:
                            if nopinc:
                                if last_dummy is not None:
                                    e.wait_ge(last_dummy[0], last_dummy[1])
                                dst = self.dummy[eng]
                                if eng == "vector":
                                    e.tensor_copy(dst[:, :], self.dummy2[:, :]) \
                                        .then_inc(inc.h, k)
                                else:
                                    e.copy(dst[:, :], self.dummy2[:, :]) \
                                        .then_inc(inc.h, k)
                                last_dummy = (inc.h, incval)
                            else:
                                r.then_inc(inc.h, k)
                return emitter

            getattr(block, eng)(mk())



def _emit_debug_p1(P, nc, dbg, kd, qd, vd, xnd, xsum_sb, s_out, s_kst, s_ve):
    dbg_w = [(s_kst[0], s_kst[0].n), (s_kst[1], s_kst[1].n), (s_ve, s_ve.n)]
    P.add("gpsimd", lambda g_: g_.dma_start(out=dbg["d_kd"], in_=kd[:, :]),
          waits=dbg_w, inc=s_out[0], k=16)
    P.add("gpsimd", lambda g_: g_.dma_start(out=dbg["d_qd"], in_=qd[:, :]), inc=s_out[0], k=16)
    P.add("gpsimd", lambda g_: g_.dma_start(out=dbg["d_vd"], in_=vd[:, :]), inc=s_out[0], k=16)
    P.add("gpsimd", lambda g_: g_.dma_start(out=dbg["d_xnd"], in_=xnd[:, :, :]), inc=s_out[0], k=16)
    P.add("gpsimd", lambda g_: g_.dma_start(out=dbg["d_xsum"], in_=xsum_sb[:, :]), inc=s_out[0], k=16)


def build_nc():
    global P2_HEADS, P2_QT
    nc = bass.Bass("TRN2", target_bir_lowering=False)

    x = nc.dram_tensor("x", [S, D], F32, kind="ExternalInput").ap()
    x_own = nc.dram_tensor("x_own", [TOK, D], F32, kind="ExternalInput").ap()
    xT = nc.dram_tensor("xT", [D, TOK], F32, kind="ExternalInput").ap()
    wq_t = nc.dram_tensor("wq_t", [128, NH * DC * 128], F32, kind="ExternalInput").ap()
    wk_t = nc.dram_tensor("wk_t", [128, NH * DC * 128], F32, kind="ExternalInput").ap()
    wv_t = nc.dram_tensor("wv_t", [128, NH * DC * 128], F32, kind="ExternalInput").ap()
    wo_t = nc.dram_tensor("wo_t", [128, DC * NH * 128], F32, kind="ExternalInput").ap()
    wg_t = nc.dram_tensor("wg_t", [HT, 128, DC * 128], BF16, kind="ExternalInput").ap()
    wu_t = nc.dram_tensor("wu_t", [HT, 128, DC * 128], BF16, kind="ExternalInput").ap()
    wd_t = nc.dram_tensor("wd_t", [DC, 128, HT * 128], BF16, kind="ExternalInput").ap()
    sg_in = nc.dram_tensor("sg_in", [128, HT], F32, kind="ExternalInput").ap()
    su_in = nc.dram_tensor("su_in", [128, HT], F32, kind="ExternalInput").ap()
    sd_in = nc.dram_tensor("sd_in", [128, DC], F32, kind="ExternalInput").ap()
    ident = nc.dram_tensor("ident", [128, 128], F32, kind="ExternalInput").ap()
    mneg_in = nc.dram_tensor("mneg_in", [128, 128], F32, kind="ExternalInput").ap()

    y = nc.dram_tensor("y", [D, TOK], F32, kind="ExternalOutput").ap()

    kd = nc.dram_tensor("kd", [NH * 128, S], BF16).ap()
    qd = nc.dram_tensor("qd", [NH * 128, S], BF16).ap()
    vd = nc.dram_tensor("vd", [NH * 128, S], BF16).ap()
    zd = nc.dram_tensor("zd", [NH * 128, S], BF16).ap()
    xnd = nc.dram_tensor("xnd", [128, DC, TOK], BF16).ap()
    spart_t = nc.dram_tensor("spart", [2, D, TOK], F32)
    sown_t = nc.dram_tensor("sown", [D, TOK], F32)
    spart = spart_t.ap()
    sown = sown_t.ap()

    dbg = {}
    if DEBUG:
        for nm, shp in [("d_kd", [NH * 128, S]), ("d_qd", [NH * 128, S]),
                        ("d_vd", [NH * 128, S]), ("d_zd", [NH * 128, S])]:
            dbg[nm] = nc.dram_tensor(nm, shp, BF16, kind="ExternalOutput").ap()
        dbg["d_xnd"] = nc.dram_tensor("d_xnd", [128, DC, TOK], BF16, kind="ExternalOutput").ap()
        dbg["d_xsum"] = nc.dram_tensor("d_xsum", [1, TOK], F32, kind="ExternalOutput").ap()
        dbg["d_sown"] = nc.dram_tensor("d_sown", [D, TOK], F32, kind="ExternalOutput").ap()

    P = Prog()
    ctx = contextlib.ExitStack()

    def sem(name, owner):
        return Sem(ctx.enter_context(nc.semaphore(name)), owner)

    s_tn = sem("s_tn", "tensor")
    s_ve = sem("s_ve", "vector")
    s_sc = sem("s_sc", "scalar")
    s_gpe = sem("s_gpe", "gpsimd")
    s_wcast = sem("s_wcast", None)
    s_x = [sem(f"s_x{i}", None) for i in range(2)]
    s_kst = [sem(f"s_kst{i}", None) for i in range(2)]
    s_ld2 = [sem(f"s_ld2{i}", None) for i in range(2)]
    s_zst = [sem(f"s_zst{i}", None) for i in range(2)]
    s_zq = [sem(f"s_zq{i}", None) for i in range(2)]
    s_spst = [sem(f"s_spst{i}", None) for i in range(2)]
    s_wo = sem("s_wo", None)
    s_cc = sem("s_cc", None)
    s_xn3 = [sem(f"s_xn3{i}", None) for i in range(2)]
    s_wgu = [sem(f"s_wgu{i}", None) for i in range(2)]
    s_wd = [sem(f"s_wd{i}", None) for i in range(2)]
    s_so = [sem(f"s_so{i}", None) for i in range(2)]
    s_xt = [sem(f"s_xt{i}", None) for i in range(2)]
    s_out = [sem(f"s_out{i}", None) for i in range(2)]
    all_sems = [s_tn, s_ve, s_sc, s_gpe, s_wcast, s_wo, s_cc] + \
        s_x + s_kst + s_ld2 + s_zst + s_zq + s_spst + s_xn3 + s_wgu + s_wd + s_so + s_xt + s_out

    def sb(name, shape, dt, stack):
        return stack.enter_context(nc.sbuf_tensor(name, shape, dt))

    def psum(name, shape, stack):
        return stack.enter_context(nc.psum_tensor(name, shape, F32))

    P.dummy = {"vector": sb("dummy_v", [128, 1], F32, ctx),
               "scalar": sb("dummy_c", [128, 1], F32, ctx)}
    P.dummy2 = sb("dummy_s", [128, 1], F32, ctx)
    idb = sb("idb", [128, 128], BF16, ctx)
    mneg = sb("mneg", [128, 128], F32, ctx)
    ones_bf = sb("ones_bf", [128, 1], BF16, ctx)
    n128_bf = sb("n128_bf", [1, 128], BF16, ctx)
    sg_sb = sb("sg_sb", [128, HT], F32, ctx)
    su_sb = sb("su_sb", [128, HT], F32, ctx)
    sd_sb = sb("sd_sb", [128, DC], F32, ctx)
    xsum_sb = sb("xsum_sb", [1, TOK], F32, ctx)
    eps_sb = sb("eps_sb", [128, 1], F32, ctx)

    P.add("gpsimd", lambda g: g.dma_start(out=idb[:, :], in_=ident[:, :]), inc=s_wcast, k=16)
    P.add("gpsimd", lambda g: g.dma_start(out=mneg[:, :], in_=mneg_in[:, :]), inc=s_wcast, k=16)
    P.add("gpsimd", lambda g: g.dma_start(out=sg_sb[:, :], in_=sg_in[:, :]), inc=s_wcast, k=16)
    P.add("gpsimd", lambda g: g.dma_start(out=su_sb[:, :], in_=su_in[:, :]), inc=s_wcast, k=16)
    P.add("gpsimd", lambda g: g.dma_start(out=sd_sb[:, :], in_=sd_in[:, :]), inc=s_wcast, k=16)
    P.add("gpsimd", lambda g: g.memset(P.dummy2[:, :], 0.0), inc=s_gpe)
    P.add("gpsimd", lambda g: g.memset(P.dummy["vector"][:, :], 0.0), inc=s_gpe)
    P.add("gpsimd", lambda g: g.memset(P.dummy["scalar"][:, :], 0.0), inc=s_gpe)
    P.add("gpsimd", lambda g: g.memset(eps_sb[:, :], 1e-5), inc=s_gpe)
    P.add("gpsimd", lambda g: g.memset(ones_bf[:, :], 1.0), inc=s_gpe)
    P.add("gpsimd", lambda g: g.memset(n128_bf[:, :], -128.0), inc=s_gpe)
    m_const = s_gpe.n

    ph1 = contextlib.ExitStack()
    wq_sb = sb("wq_sb", [128, NH * DC * 128], BF16, ph1)
    wk_sb = sb("wk_sb", [128, NH * DC * 128], BF16, ph1)
    wv_sb = sb("wv_sb", [128, NH * DC * 128], BF16, ph1)
    x_t = [sb(f"x_t{i}", [128, D], F32, ph1) for i in range(2)]
    sum_c = [sb(f"sum_c{i}", [128, 1], F32, ph1) for i in range(2)]
    ssum_c = [sb(f"ssum_c{i}", [128, 1], F32, ph1) for i in range(2)]
    mean_c = [sb(f"mean_c{i}", [128, 1], F32, ph1) for i in range(2)]
    msq_c = [sb(f"msq_c{i}", [128, 1], F32, ph1) for i in range(2)]
    var_c = [sb(f"var_c{i}", [128, 1], F32, ph1) for i in range(2)]
    sdv = [sb(f"sdv{i}", [128, 1], F32, ph1) for i in range(2)]
    rstd = [sb(f"rstd{i}", [128, 1], F32, ph1) for i in range(2)]
    nmr = [sb(f"nmr{i}", [128, 1], F32, ph1) for i in range(2)]
    xn_t = [sb(f"xn_t{i}", [128, D], BF16, ph1) for i in range(2)]
    xn_fm = [sb(f"xn_fm{i}", [128, DC * 512], BF16, ph1) for i in range(2)]
    kvq_st = [sb(f"kvq_st{i}", [128, 512], BF16, ph1) for i in range(2)]
    xs_st = sb("xs_st", [1, 512], F32, ph1)
    ps_xp = psum("ps_xp", [128, 1024], ph1)
    ps_kv = [psum(f"ps_kv{i}", [128, 512], ph1) for i in range(2)]
    ps_xs = psum("ps_xs", [128, 512], ph1)

    P.add("gpsimd", lambda g: g.dma_start(out=wq_sb[:, :], in_=wq_t[:, :]), inc=s_wcast, k=16)
    P.add("gpsimd", lambda g: g.dma_start(out=wk_sb[:, :], in_=wk_t[:, :]), inc=s_wcast, k=16)
    P.add("gpsimd", lambda g: g.dma_start(out=wv_sb[:, :], in_=wv_t[:, :]), inc=s_wcast, k=16)
    m_wcast = s_wcast.n

    m_xn = {}
    m_tp = {}
    m_xnf = {}
    m_kvmm = {}
    m_kvev = {}
    m_fm_wait = {0: [], 1: []}

    def ln_tile(src_ap, pt, lin):
        par = lin % 2
        P.add("sync",
              lambda e, d=src_ap, par=par: e.dma_start(out=x_t[par][:, :], in_=d),
              waits=[(s_ve, m_xn.get(lin - 2, 0))], inc=s_x[par], k=16)
        mark_ld = s_x[par].n
        P.add("vector",
              lambda e, par=par: e.tensor_scalar(xn_t[par][:, :], x_t[par][:, :], 1.0, None,
                                                 ALU.mult, ALU.add, accum_out=sum_c[par][:, :]),
              waits=[(s_x[par], mark_ld), (s_gpe, 3)], inc=s_ve, nopinc=True)
        m_sum = s_ve.n
        P.add("scalar",
              lambda e, par=par: e.activation(xn_t[par][:, :], x_t[par][:, :], ACTF.Square,
                                              accum_out=ssum_c[par][:, :]),
              waits=[(s_ve, m_sum)])
        P.add("scalar", lambda e, par=par: e.mul(mean_c[par][:, :], sum_c[par][:, :], 1.0 / D),
              inc=s_sc)
        m_stat1 = s_sc.n
        P.add("vector",
              lambda e, par=par: e.tensor_tensor(msq_c[par][:, :], mean_c[par][:, :],
                                                 mean_c[par][:, :], ALU.mult),
              waits=[(s_sc, m_stat1)], inc=s_ve)
        m_msq = s_ve.n
        P.add("vector",
              lambda e, par=par: e.scalar_tensor_tensor(var_c[par][:, :], ssum_c[par][:, :],
                                                        1.0 / D, msq_c[par][:, :],
                                                        ALU.mult, ALU.subtract),
              waits=[(s_ve, m_msq)], inc=s_ve)
        m_var = s_ve.n
        P.add("scalar",
              lambda e, par=par: e.activation(sdv[par][:, :], var_c[par][:, :], ACTF.Sqrt,
                                              bias=eps_sb[:, :], scale=1.0),
              waits=[(s_ve, m_var), (s_gpe, 4)], inc=s_sc)
        m_sdv = s_sc.n
        P.add("vector", lambda e, par=par: e.reciprocal(rstd[par][:, :], sdv[par][:, :]),
              waits=[(s_sc, m_sdv)], inc=s_ve)
        m_rstd = s_ve.n
        P.add("vector",
              lambda e, par=par: e.tensor_scalar(nmr[par][:, :], mean_c[par][:, :],
                                                 rstd[par][:, :], -1.0, ALU.mult, ALU.mult),
              waits=[(s_ve, m_rstd)], inc=s_ve)
        m_nmr = s_ve.n
        P.add("vector",
              lambda e, par=par: e.tensor_scalar(xn_t[par][:, :], x_t[par][:, :],
                                                 rstd[par][:, :], nmr[par][:, :],
                                                 ALU.mult, ALU.add),
              waits=[(s_ve, m_nmr)], inc=s_ve)
        m_xn[lin] = s_ve.n
        def tps(e, par=par):
            r = None
            for c in range(DC):
                r = e.transpose(ps_xp.bitcast(BF16)[:, c * 128:(c + 1) * 128],
                                xn_t[par][:, c * 128:(c + 1) * 128], idb[:, :])
            return r
        P.add("tensor", tps,
              waits=[(s_ve, m_xn[lin]), (s_ve, m_xnf.get(lin - 1, 0)),
                     (s_wcast, s_wcast.n)],
              inc=s_tn)
        m_tp[lin] = s_tn.n
        return par

    def xnf_evac(gpar, sub, lin):
        sub128 = sub * 128
        def ev(e, gpar=gpar, sub128=sub128):
            out_ap = xn_fm[gpar][:, 0:DC * 512].rearrange("p (c t) -> p c t", c=DC)[:, :, sub128:sub128 + 128]
            in_ap = ps_xp.bitcast(BF16)[:, 0:2048].rearrange("p (c t) -> p c t", c=DC)
            return e.tensor_copy(out_ap, in_ap)
        P.add("vector", ev, waits=[(s_tn, m_tp[lin])] + m_fm_wait[gpar], inc=s_ve)
        m_xnf[lin] = s_ve.n

    mats = [(wq_sb, qd), (wk_sb, kd), (wv_sb, vd)]
    kv_i2 = 0
    for g in range(NG):
        gpar = g % 2
        for sub in range(4):
            pt = g * 4 + sub
            ln_tile(x[pt * 128:(pt + 1) * 128, :], pt, pt)
            xnf_evac(gpar, sub, pt)
        m_grp = m_xnf[g * 4 + 3]
        for j in range(NH):
            for mi, (w_sb, dest) in enumerate(mats):
                i2 = kv_i2 % 2
                kv_i2 += 1
                def qkv(e, w_sb=w_sb, j=j, i2=i2, gpar=gpar):
                    r = None
                    for c in range(DC):
                        off = (j * DC + c) * 128
                        r = e.matmul(ps_kv[i2][:, :], w_sb[:, off:off + 128],
                                     xn_fm[gpar][:, c * 512:(c + 1) * 512],
                                     start=(c == 0), stop=(c == DC - 1))
                    return r
                P.add("tensor", qkv,
                      waits=[(s_ve, m_grp), (s_wcast, m_wcast),
                             (s_sc, m_kvev.get(kv_i2 - 3, 0))],
                      inc=s_tn)
                m_kvmm[kv_i2 - 1] = s_tn.n
                P.add("scalar",
                      lambda e, i2=i2: e.activation(kvq_st[i2][:, :], ps_kv[i2][:, :], ACTF.Copy),
                      waits=[(s_tn, m_kvmm[kv_i2 - 1]), (s_kst[i2], 16 * max(0, (kv_i2 - 1) // 2 + 1 - 1))],
                      inc=s_sc)
                m_kvev[kv_i2 - 1] = s_sc.n
                P.add("gpsimd",
                      lambda g_, dest=dest, j=j, g=g, i2=i2: g_.dma_start(
                          out=dest[j * 128:(j + 1) * 128, g * 512:(g + 1) * 512],
                          in_=kvq_st[i2][:, :]),
                      waits=[(s_sc, m_kvev[kv_i2 - 1])], inc=s_kst[i2], k=16)
        m_fm_wait[gpar] = [(s_tn, s_tn.n)]

    m_xsev_prev = 0
    for g2 in range(4):
        gpar = g2 % 2
        for sub in range(4):
            lin = NT + g2 * 4 + sub
            pt2 = g2 * 4 + sub
            ln_tile(x_own[pt2 * 128:(pt2 + 1) * 128, :], pt2, lin)
            xnf_evac(gpar, sub, lin)
        m_grp = m_xnf[NT + g2 * 4 + 3]
        P.add("gpsimd",
              lambda g_, gpar=gpar, g2=g2: g_.dma_start(
                  out=xnd[:, :, g2 * 512:(g2 + 1) * 512],
                  in_=xn_fm[gpar][:, 0:DC * 512].rearrange("p (c t) -> p c t", c=DC)),
              waits=[(s_ve, m_grp)], inc=s_kst[gpar], k=16)
        def xs_mm(e, gpar=gpar):
            r = None
            for c in range(DC):
                r = e.matmul(ps_xs[0:1, :], ones_bf[:, :],
                             xn_fm[gpar][:, c * 512:(c + 1) * 512],
                             start=(c == 0), stop=(c == DC - 1))
            return r
        P.add("tensor", xs_mm,
              waits=[(s_ve, m_grp), (s_gpe, m_const), (s_ve, m_xsev_prev)], inc=s_tn)
        m_fm_wait[gpar] = [(s_tn, s_tn.n), (s_kst[gpar], s_kst[gpar].n)]
        m_xs = s_tn.n
        P.add("vector", lambda e, g2=g2: e.tensor_copy(xsum_sb[:, g2 * 512:(g2 + 1) * 512],
                                                       ps_xs[0:1, :]),
              waits=[(s_tn, m_xs)], inc=s_ve)
        m_xsev_prev = s_ve.n
    m_xsum_done = s_ve.n
    m_ph1_stores = [(s_kst[0], s_kst[0].n), (s_kst[1], s_kst[1].n)]

    ph1.close()
    P.fence(all_sems)

    if STAGE < 2:
        fin = s_out[0].n
        if DEBUG and DEBUG_DUMPS:
            _emit_debug_p1(P, nc, dbg, kd, qd, vd, xnd, xsum_sb, s_out, s_kst, s_ve)
            fin = s_out[0].n
        P.add("gpsimd", lambda g_: g_.nop(), waits=[(s_out[0], fin)] if fin else [(s_gpe, s_gpe.n)])
        with nc.Block() as block:
            P.emit(block)
        ctx.close()
        return nc

    ph2 = contextlib.ExitStack()
    k_sb = [sb(f"k_sb{i}", [128, S], BF16, ph2) for i in range(2)]
    q_sb = [sb(f"q_sb{i}", [128, S], BF16, ph2) for i in range(2)]
    v_raw = [sb(f"v_raw{i}", [128, S], BF16, ph2) for i in range(2)]
    v_sb = [sb(f"v_sb{i}", [128, S], BF16, ph2) for i in range(2)]
    s_neg = [sb(f"s_neg{i}", [128, S], F32, ph2) for i in range(2)]
    p_sb = [sb(f"p_sb{i}", [128, S], BF16, ph2) for i in range(2)]
    pt_sb = sb("pt_sb", [128, NT * 512], BF16, ph2)
    nm_cols = sb("nm_cols", [128, 8], F32, ph2)
    nm_safe = [sb(f"nm_safe{i}", [128, 8], F32, ph2) for i in range(2)]
    negm_b = sb("negm_b", [128, 64], F32, ph2)
    den_b = sb("den_b", [128, 64], F32, ph2)
    rden_b = sb("rden_b", [128, 64], F32, ph2)
    z_st = [sb(f"z_st{i}", [128, 512], BF16, ph2) for i in range(2)]
    spacer = sb("spacer", [128, 1], F32, ph2)
    ps_s = [psum(f"ps_s{i}", [128, 512], ph2) for i in range(2)]
    ps_pt = [psum(f"ps_pt{i}", [128, 512], ph2) for i in range(2)]
    ps_z = psum("ps_z", [128, 512], ph2)
    ps_vt = [psum(f"ps_vt{i}", [128, 512], ph2) for i in range(2)]

    m_exp = {}
    m_norm = {}
    m_smm = {}
    m_sevac = {}
    m_ptb = {}
    m_ptevac = {}
    m_negm = {}
    m_av = {}
    m_zevac = {}
    m_vtb = {}
    m_vte = {}
    m_headlast = {}
    sbank = 0
    ptb_cnt = 0
    vt_cnt = 0

    def col(n, qt):
        return ((n // NT) % 2) * 32 + qt

    for j in range(P2_HEADS):
        par = j % 2
        w_ld = [(s_tn, m_headlast.get(j - 2, 0))] + (m_ph1_stores if j < 2 else [])
        P.add("sync", lambda e, j=j, par=par: e.dma_start(out=k_sb[par][:, :],
                                                          in_=kd[j * 128:(j + 1) * 128, :]),
              waits=w_ld, inc=s_ld2[par], k=16)
        P.add("sync", lambda e, j=j, par=par: e.dma_start(out=q_sb[par][:, :],
                                                          in_=qd[j * 128:(j + 1) * 128, :]),
              inc=s_ld2[par], k=16)
        P.add("sync", lambda e, j=j, par=par: e.dma_start(out=v_raw[par][:, :],
                                                          in_=vd[j * 128:(j + 1) * 128, :]),
              inc=s_ld2[par], k=16)
        m_ld = s_ld2[par].n
        for r in range(4):
            vpar = vt_cnt % 2
            vt_cnt += 1
            def vtp(e, par=par, r=r, vpar=vpar):
                rr = None
                for t in range(8):
                    kb = r * 8 + t
                    rr = e.transpose(ps_vt[vpar].bitcast(BF16)[:, t * 128:(t + 1) * 128],
                                     v_raw[par][:, kb * 128:(kb + 1) * 128], idb[:, :])
                return rr
            P.add("tensor", vtp,
                  waits=[(s_ld2[par], m_ld), (s_ve, m_vte.get(vt_cnt - 3, 0))], inc=s_tn)
            m_vtb[vt_cnt - 1] = s_tn.n
            P.add("vector",
                  lambda e, r=r, vpar=vpar, par=par: e.tensor_copy(
                      v_sb[par][:, r * 1024:(r + 1) * 1024],
                      ps_vt[vpar].bitcast(BF16)[:, 0:1024]),
                  waits=[(s_tn, m_vtb[vt_cnt - 1]), (s_tn, m_av.get((j - 2, 7), 0))],
                  inc=s_ve)
            m_vte[vt_cnt - 1] = s_ve.n
        m_vte_head = s_ve.n

        for qt in range(P2_QT):
            n = j * NT + qt
            npar = n % 2
            kext = (qt + 1) * 128
            nkc = (kext + 511) // 512
            for kc in range(nkc):
                N = min(512, kext - kc * 512)
                bank = sbank % 2
                sbank += 1
                sb_idx = sbank - 1
                P.add("tensor",
                      lambda e, par=par, qt=qt, kc=kc, N=N, bank=bank: e.matmul(
                          ps_s[bank][:, 0:N], q_sb[par][:, qt * 128:(qt + 1) * 128],
                          k_sb[par][:, kc * 512:kc * 512 + N], start=True, stop=True),
                      waits=[(s_ld2[par], m_ld), (s_ve, m_sevac.get(sb_idx - 2, 0))],
                      inc=s_tn)
                m_smm[sb_idx] = s_tn.n
                w = [(s_tn, m_smm[sb_idx])]
                if kc == 0:
                    w.append((s_sc, m_exp.get(n - 2, 0)))
                    w.append((s_gpe, s_gpe.n))
                if kc == nkc - 1:
                    P.add("vector",
                          lambda e, bank=bank, N=N: e.tensor_tensor(
                              ps_s[bank][:, N - 128:N], ps_s[bank][:, N - 128:N],
                              mneg[:, :], ALU.add),
                          waits=list(w), inc=s_ve)
                    w = [(s_ve, s_ve.n)]
                P.add("vector",
                      lambda e, npar=npar, kc=kc, N=N, bank=bank: e.tensor_scalar(
                          s_neg[npar][:, kc * 512:kc * 512 + N], ps_s[bank][:, 0:N],
                          -SCALE, None, ALU.mult, ALU.min, accum_out=nm_cols[:, kc:kc + 1]),
                      waits=w, inc=s_ve, nopinc=True)
                m_sevac[sb_idx] = s_ve.n
            if n > 0:
                pn, pqt, pj = n - 1, (n - 1) % NT, (n - 1) // NT
                pkext = (pqt + 1) * 128
                P.add("vector",
                      lambda e, pj=pj, pqt=pqt, pn=pn: e.reciprocal(
                          rden_b[:, col(pn, pqt):col(pn, pqt) + 1],
                          den_b[:, col(pn, pqt):col(pn, pqt) + 1]),
                      waits=[(s_sc, m_exp[pn])], inc=s_ve)
                m_rd = s_ve.n
                P.add("vector",
                      lambda e, pn=pn, pqt=pqt, pkext=pkext: e.tensor_scalar(
                          p_sb[pn % 2][:, 0:pkext], p_sb[pn % 2][:, 0:pkext],
                          rden_b[:, col(pn, pqt):col(pn, pqt) + 1], None, ALU.mult),
                      waits=[(s_ve, m_rd)], inc=s_ve)
                m_norm[pn] = s_ve.n
            else:
                P.add("vector", lambda e: e.tensor_copy(spacer[:, :], mneg[:, 0:1]))
            m_evac_last = s_ve.n
            P.add("gpsimd",
                  lambda e, n=n, nkc=nkc: e.tensor_copy(
                      nm_safe[n % 2][:, 0:nkc], nm_cols[:, 0:nkc]),
                  waits=[(s_ve, m_evac_last)], inc=s_gpe)
            m_nmsafe = s_gpe.n
            P.add("vector",
                  lambda e, n=n, qt=qt, nkc=nkc: e.tensor_reduce(
                      negm_b[:, col(n, qt):col(n, qt) + 1], nm_safe[n % 2][:, 0:nkc],
                      axis=AX.X, op=ALU.min),
                  waits=[(s_gpe, m_nmsafe)], inc=s_ve)
            m_negm[n] = s_ve.n
            P.add("scalar",
                  lambda e, n=n, qt=qt, npar=npar, kext=kext: e.activation(
                      p_sb[npar][:, 0:kext], s_neg[npar][:, 0:kext], ACTF.Exp,
                      bias=negm_b[:, col(n, qt):col(n, qt) + 1], scale=-1.0,
                      accum_out=den_b[:, col(n, qt):col(n, qt) + 1]),
                  waits=[(s_ve, m_negm[n]), (s_tn, m_ptb.get(("last", n - 2), 0))],
                  inc=s_sc, nopinc=True)
            m_exp[n] = s_sc.n
            if n > 0:
                emit_tp_av = [n - 1]
            else:
                emit_tp_av = []
            for en in emit_tp_av:
                eqt, ej = en % NT, en // NT
                nt_tiles = eqt + 1
                for b0 in range(0, nt_tiles, 8):
                    bcnt = min(8, nt_tiles - b0)
                    bank = ptb_cnt % 2
                    ptb_cnt += 1
                    def ptp(e, en=en, b0=b0, bcnt=bcnt, bank=bank):
                        rr = None
                        for t in range(bcnt):
                            rr = e.transpose(ps_pt[bank].bitcast(BF16)[:, t * 128:(t + 1) * 128],
                                             p_sb[en % 2][:, (b0 + t) * 128:(b0 + t + 1) * 128],
                                             idb[:, :])
                        return rr
                    P.add("tensor", ptp,
                          waits=[(s_ve, m_norm[en]), (s_ve, m_ptevac.get(ptb_cnt - 3, 0))],
                          inc=s_tn)
                    m_ptb[ptb_cnt - 1] = s_tn.n
                    def ptev(e, en=en, b0=b0, bcnt=bcnt, bank=bank):
                        eqt = en % NT
                        out_ap = pt_sb[:, 0:NT * 512] \
                            .rearrange("p (kb rest) -> p kb rest", rest=512)[:, b0:b0 + bcnt, (eqt % 4) * 128:(eqt % 4) * 128 + 128]
                        in_ap = ps_pt[bank].bitcast(BF16)[:, 0:bcnt * 128] \
                            .rearrange("p (kb t) -> p kb t", t=128)
                        return e.tensor_copy(out_ap, in_ap)
                    P.add("vector", ptev, waits=[(s_tn, m_ptb[ptb_cnt - 1])], inc=s_ve)
                    m_ptevac[ptb_cnt - 1] = s_ve.n
                m_ptb[("last", en)] = m_ptb[ptb_cnt - 1]
                if eqt % 4 == 3:
                    gi = eqt // 4
                    m_pt_ready = s_ve.n
                    def av(e, ej=ej, gi=gi):
                        rr = None
                        for kb in range(4 * gi + 4):
                            off = max(0, kb - 4 * gi) * 128
                            rr = e.matmul(ps_z[:, off:512],
                                          v_sb[ej % 2][:, kb * 128:(kb + 1) * 128],
                                          pt_sb[:, kb * 512 + off:kb * 512 + 512],
                                          start=(kb == 0), stop=(kb == 4 * gi + 3))
                        return rr
                    prevz = (ej, gi - 1) if gi > 0 else (ej - 1, 7)
                    P.add("tensor", av,
                          waits=[(s_ve, m_pt_ready), (s_ve, m_vte_head),
                                 (s_sc, m_zevac.get(prevz, 0))],
                          inc=s_tn)
                    m_av[(ej, gi)] = s_tn.n
                    zpar = gi % 2
                    P.add("scalar",
                          lambda e, zpar=zpar: e.activation(z_st[zpar][:, :], ps_z[:, :], ACTF.Copy),
                          waits=[(s_tn, m_av[(ej, gi)]), (s_zst[zpar], 16 * max(0, ((ej * 8 + gi) // 2 + 1) - 1))],
                          inc=s_sc)
                    m_zevac[(ej, gi)] = s_sc.n
                    P.add("sync",
                          lambda g_, ej=ej, gi=gi, zpar=zpar: g_.dma_start(
                              out=zd[ej * 128:(ej + 1) * 128, gi * 512:(gi + 1) * 512],
                              in_=z_st[zpar][:, :]),
                          waits=[(s_sc, m_zevac[(ej, gi)])], inc=s_zst[zpar], k=16)
        m_headlast[j] = s_tn.n

    if P2_HEADS < NH or P2_QT < NQT:
        fin = s_out[0].n
        if DEBUG and DEBUG_DUMPS:
            d2w = [(s_ve, s_ve.n), (s_sc, s_sc.n), (s_tn, s_tn.n)]
            for nm, src_ap, shp, dt in [
                    ("d_vsb", v_sb[:, :], [128, S], BF16),
                    ("d_ptsb", pt_sb[:, :], [128, NT * 512], BF16),
                    ("d_psb0", p_sb[0][:, :], [128, S], BF16),
                    ("d_psb1", p_sb[1][:, :], [128, S], BF16),
                    ("d_sneg0", s_neg[0][:, :], [128, S], F32),
                    ("d_sneg1", s_neg[1][:, :], [128, S], F32),
                    ("d_negm", negm_b[:, :], [128, 64], F32),
                    ("d_den", den_b[:, :], [128, 64], F32)]:
                ap = nc.dram_tensor(nm, shp, dt, kind="ExternalOutput").ap()
                P.add("gpsimd", lambda g_, ap=ap, src_ap=src_ap: g_.dma_start(out=ap, in_=src_ap),
                      waits=d2w, inc=s_out[0], k=16)
                d2w = []
        if DEBUG and DEBUG_DUMPS:
            _emit_debug_p1(P, nc, dbg, kd, qd, vd, xnd, xsum_sb, s_out, s_kst, s_ve)
            dbg_w = [(s_zst[0], s_zst[0].n), (s_zst[1], s_zst[1].n), (s_sc, s_sc.n)]
            P.add("gpsimd", lambda g_: g_.dma_start(out=dbg["d_zd"], in_=zd[:, :]),
                  waits=dbg_w, inc=s_out[0], k=16)
            fin = s_out[0].n
        P.add("gpsimd", lambda g_: g_.nop(), waits=[(s_out[0], fin)] if fin else [(s_gpe, s_gpe.n)])
        ph2.close()
        with nc.Block() as block:
            P.emit(block)
        ctx.close()
        return nc
    n = NH * NT - 1
    qt, j = n % NT, n // NT
    kext = (qt + 1) * 128
    P.add("vector", lambda e, n=n, qt=qt: e.reciprocal(
        rden_b[:, col(n, qt):col(n, qt) + 1], den_b[:, col(n, qt):col(n, qt) + 1]),
        waits=[(s_sc, m_exp[n])], inc=s_ve)
    m_rd = s_ve.n
    P.add("vector", lambda e, n=n, qt=qt, kext=kext: e.tensor_scalar(
        p_sb[n % 2][:, 0:kext], p_sb[n % 2][:, 0:kext],
        rden_b[:, col(n, qt):col(n, qt) + 1], None, ALU.mult),
        waits=[(s_ve, m_rd)], inc=s_ve)
    m_norm[n] = s_ve.n
    nt_tiles = qt + 1
    for b0 in range(0, nt_tiles, 8):
        bcnt = min(8, nt_tiles - b0)
        bank = ptb_cnt % 2
        ptb_cnt += 1
        def ptp(e, n=n, b0=b0, bcnt=bcnt, bank=bank):
            rr = None
            for t in range(bcnt):
                rr = e.transpose(ps_pt[bank].bitcast(BF16)[:, t * 128:(t + 1) * 128],
                                 p_sb[n % 2][:, (b0 + t) * 128:(b0 + t + 1) * 128], idb[:, :])
            return rr
        P.add("tensor", ptp,
              waits=[(s_ve, m_norm[n]), (s_ve, m_ptevac.get(ptb_cnt - 3, 0))], inc=s_tn)
        m_ptb[ptb_cnt - 1] = s_tn.n
        def ptev(e, n=n, b0=b0, bcnt=bcnt, bank=bank):
            qt = n % NT
            out_ap = pt_sb[:, 0:NT * 512] \
                .rearrange("p (kb rest) -> p kb rest", rest=512)[:, b0:b0 + bcnt, (qt % 4) * 128:(qt % 4) * 128 + 128]
            in_ap = ps_pt[bank].bitcast(BF16)[:, 0:bcnt * 128] \
                .rearrange("p (kb t) -> p kb t", t=128)
            return e.tensor_copy(out_ap, in_ap)
        P.add("vector", ptev, waits=[(s_tn, m_ptb[ptb_cnt - 1])], inc=s_ve)
        m_ptevac[ptb_cnt - 1] = s_ve.n
    m_pt_ready = s_ve.n
    def av_last(e):
        rr = None
        for kb in range(NT):
            off = max(0, kb - 28) * 128
            rr = e.matmul(ps_z[:, off:512], v_sb[(NH - 1) % 2][:, kb * 128:(kb + 1) * 128],
                          pt_sb[:, kb * 512 + off:kb * 512 + 512],
                          start=(kb == 0), stop=(kb == NT - 1))
        return rr
    P.add("tensor", av_last,
          waits=[(s_ve, m_pt_ready), (s_sc, m_zevac.get((NH - 1, 6), 0))], inc=s_tn)
    m_av[(NH - 1, 7)] = s_tn.n
    P.add("scalar", lambda e: e.activation(z_st[1][:, :], ps_z[:, :], ACTF.Copy),
          waits=[(s_tn, m_av[(NH - 1, 7)]), (s_zst[1], s_zst[1].n)], inc=s_sc)
    m_zevac[(NH - 1, 7)] = s_sc.n
    P.add("sync", lambda g_: g_.dma_start(out=zd[(NH - 1) * 128:NH * 128, 7 * 512:8 * 512],
                                          in_=z_st[1][:, :]),
          waits=[(s_sc, m_zevac[(NH - 1, 7)])], inc=s_zst[1], k=16)
    m_zd_done = [(s_zst[0], s_zst[0].n), (s_zst[1], s_zst[1].n)]

    ph2.close()
    P.fence(all_sems)

    if STAGE < 3:
        fin = s_out[0].n
        if DEBUG and DEBUG_DUMPS:
            _emit_debug_p1(P, nc, dbg, kd, qd, vd, xnd, xsum_sb, s_out, s_kst, s_ve)
            dbg_w = [(s_zst[0], s_zst[0].n), (s_zst[1], s_zst[1].n)]
            P.add("gpsimd", lambda g_: g_.dma_start(out=dbg["d_zd"], in_=zd[:, :]),
                  waits=dbg_w, inc=s_out[0], k=16)
            fin = s_out[0].n
        P.add("gpsimd", lambda g_: g_.nop(), waits=[(s_out[0], fin)] if fin else [(s_gpe, s_gpe.n)])
        with nc.Block() as block:
            P.emit(block)
        ctx.close()
        return nc

    ph25 = contextlib.ExitStack()
    wo_sb = sb("wo_sb", [128, DC * NH * 128], BF16, ph25)
    zq_sb = [sb(f"zq_sb{i}", [128, NH * 512], BF16, ph25) for i in range(2)]
    sp_st = [sb(f"sp_st{i}", [128, 512], F32, ph25) for i in range(2)]
    ps_wo = [psum(f"ps_wo{i}", [128, 512], ph25) for i in range(2)]

    P.add("gpsimd", lambda g_: g_.dma_start(out=wo_sb[:, :], in_=wo_t[:, :]),
          waits=list(m_zd_done), inc=s_wo, k=16)
    m_womm = {}
    m_spev = {}
    spcnt = 0
    for g in range(NG):
        gpar = g % 2
        def zql(e, g=g, gpar=gpar):
            in_ap = zd[:, g * 512:(g + 1) * 512].rearrange("(j p) t -> p j t", p=128)
            return e.dma_start(out=zq_sb[gpar][:, 0:NH * 512]
                               .rearrange("p (j t) -> p j t", j=NH), in_=in_ap)
        P.add("sync", zql,
              waits=list(m_zd_done) + [(s_tn, m_womm.get((g - 2, DC - 1), 0))],
              inc=s_zq[gpar], k=16)
        m_zq = s_zq[gpar].n
        for dt in range(DC):
            bank = spcnt % 2
            spcnt += 1
            def womm(e, g=g, dt=dt, bank=bank, gpar=gpar):
                rr = None
                for jj in range(NH):
                    rr = e.matmul(ps_wo[bank][:, :],
                                  wo_sb[:, (dt * NH + jj) * 128:(dt * NH + jj + 1) * 128],
                                  zq_sb[gpar][:, jj * 512:(jj + 1) * 512],
                                  start=(jj == 0), stop=(jj == NH - 1))
                return rr
            P.add("tensor", womm,
                  waits=[(s_zq[gpar], m_zq), (s_wo, 16), (s_ve, m_spev.get(spcnt - 3, 0))],
                  inc=s_tn)
            m_womm[(g, dt)] = s_tn.n
            P.add("vector",
                  lambda e, bank=bank: e.tensor_copy(sp_st[bank][:, :], ps_wo[bank][:, :]),
                  waits=[(s_tn, m_womm[(g, dt)]), (s_spst[bank], 16 * max(0, (spcnt - 1) // 2))],
                  inc=s_ve)
            m_spev[spcnt - 1] = s_ve.n
            P.add("gpsimd",
                  lambda g_, g=g, dt=dt, bank=bank: g_.dma_start(
                      out=spart[g // 4, dt * 128:(dt + 1) * 128,
                                (g % 4) * 512:(g % 4 + 1) * 512],
                      in_=sp_st[bank][:, :]),
                  waits=[(s_ve, m_spev[spcnt - 1])], inc=s_spst[bank], k=16)
    m_sp_done = [(s_spst[0], s_spst[0].n), (s_spst[1], s_spst[1].n)]
    P.add("gpsimd",
          lambda g_: g_.collective_compute(
              "ReduceScatter", ALU.add,
              replica_groups=[[0, 1], [2, 3], [4, 5], [6, 7]],
              ins=[spart_t.ap().opt()], outs=[sown_t.ap().opt()]),
          waits=list(m_sp_done), inc=s_cc)

    ph25.close()
    P.fence([s for s in all_sems if s is not s_cc])

    if STAGE < 4:
        fin = s_out[0].n
        if DEBUG and DEBUG_DUMPS:
            _emit_debug_p1(P, nc, dbg, kd, qd, vd, xnd, xsum_sb, s_out, s_kst, s_ve)
            dbg_w = [(s_zst[0], s_zst[0].n), (s_zst[1], s_zst[1].n)]
            P.add("gpsimd", lambda g_: g_.dma_start(out=dbg["d_zd"], in_=zd[:, :]),
                  waits=dbg_w, inc=s_out[0], k=16)
            P.add("gpsimd", lambda g_: g_.dma_start(out=dbg["d_sown"], in_=sown[:, :]),
                  waits=[(s_cc, 1)], inc=s_out[0], k=16)
            fin = s_out[0].n
        P.add("gpsimd", lambda g_: g_.nop(), waits=[(s_out[0], fin)] if fin else [(s_gpe, s_gpe.n)])
        with nc.Block() as block:
            P.emit(block)
        ctx.close()
        return nc

    ph3 = contextlib.ExitStack()
    xn3 = [sb(f"xn3{i}", [128, DC * 512], BF16, ph3) for i in range(2)]
    hg_sb = sb("hg_sb", [128, HT * 512], BF16, ph3)
    wg_sb = [sb(f"wg_sb{i}", [128, DC * 128], BF16, ph3) for i in range(2)]
    wu_sb = [sb(f"wu_sb{i}", [128, DC * 128], BF16, ph3) for i in range(2)]
    wd_sb = [sb(f"wd_sb{i}", [128, HT * 128], BF16, ph3) for i in range(2)]
    gate_b = [sb(f"gate_b{i}", [128, 512], BF16, ph3) for i in range(2)]
    up_b = [sb(f"up_b{i}", [128, 512], BF16, ph3) for i in range(2)]
    xs_hi = [sb(f"xs_hi{i}", [1, 512], BF16, ph3) for i in range(2)]
    xs_lo = [sb(f"xs_lo{i}", [1, 512], BF16, ph3) for i in range(2)]
    hs32 = sb("hs32", [1, 512], F32, ph3)
    hs_hi = [sb(f"hs_hi{i}", [1, 512], BF16, ph3) for i in range(2)]
    hs_lo = [sb(f"hs_lo{i}", [1, 512], BF16, ph3) for i in range(2)]
    g_sb = [sb(f"g_sb{i}", [128, 512], F32, ph3) for i in range(2)]
    sgf = [sb(f"sgf{i}", [128, 512], F32, ph3) for i in range(2)]
    so_sb = [sb(f"so_sb{i}", [128, 512], F32, ph3) for i in range(2)]
    xt_sb = [sb(f"xt_sb{i}", [128, 512], F32, ph3) for i in range(2)]
    out_st = [sb(f"out_st{i}", [128, 512], F32, ph3) for i in range(2)]
    ps_g = [psum(f"ps_g{i}", [128, 512], ph3) for i in range(2)]
    ps_u = [psum(f"ps_u{i}", [128, 512], ph3) for i in range(2)]
    ps_hs = psum("ps_hs", [128, 512], ph3)
    ps_d = [psum(f"ps_d{i}", [128, 512], ph3) for i in range(2)]

    m_gmm = {}
    m_umm = {}
    m_gsil = {}
    m_uev = {}
    m_hg = {}
    m_hsmm = {}
    m_hs = {}
    m_dmm = {}
    m_gf = {}
    m_sg = {}
    m_out = {}
    n_wgu = 0
    n_wd = 0
    n_out = 0

    for qi in range(4):
        qpar = qi % 2
        P.add("sync",
              lambda e, qi=qi, qpar=qpar: e.dma_start(
                  out=xn3[qpar][:, 0:DC * 512].rearrange("p (c t) -> p c t", c=DC),
                  in_=xnd[:, :, qi * 512:(qi + 1) * 512]),
              waits=[(s_tn, m_gmm.get((qi - 2, HT - 1), 0))], inc=s_xn3[qpar], k=16)
        m_x3 = s_xn3[qpar].n
        P.add("vector",
              lambda e, qi=qi, qpar=qpar: e.tensor_copy(xs_hi[qpar][:, :],
                                                        xsum_sb[:, qi * 512:(qi + 1) * 512]),
              waits=[(s_ve, m_xsum_done), (s_tn, m_umm.get((qi - 2, HT - 1), 0))], inc=s_ve)
        m_xh = s_ve.n
        P.add("vector",
              lambda e, qi=qi, qpar=qpar: e.tensor_tensor(
                  xs_lo[qpar][:, :], xsum_sb[:, qi * 512:(qi + 1) * 512],
                  xs_hi[qpar][:, :], ALU.subtract),
              waits=[(s_ve, m_xh)], inc=s_ve)
        m_xs = s_ve.n
        for ht in range(HT):
            hpar = ht % 2
            n_wgu += 1
            P.add("sync",
                  lambda e, ht=ht, hpar=hpar: e.dma_start(out=wg_sb[hpar][:, :], in_=wg_t[ht, :, :]),
                  waits=[(s_tn, m_umm.get((qi, ht - 2) if ht >= 2 else (qi - 1, HT - 2 + ht), 0))],
                  inc=s_wgu[hpar], k=16)
            P.add("sync",
                  lambda e, ht=ht, hpar=hpar: e.dma_start(out=wu_sb[hpar][:, :], in_=wu_t[ht, :, :]),
                  inc=s_wgu[hpar], k=16)
            m_w = s_wgu[hpar].n
            def gmm(e, qpar=qpar, hpar=hpar):
                for c in range(DC):
                    e.matmul(ps_g[hpar][:, :], wg_sb[hpar][:, c * 128:(c + 1) * 128],
                             xn3[qpar][:, c * 512:(c + 1) * 512],
                             start=(c == 0), stop=False)
                e.matmul(ps_g[hpar][:, :], n128_bf[:, :], xs_hi[qpar][:, :],
                         start=False, stop=False)
                return e.matmul(ps_g[hpar][:, :], n128_bf[:, :], xs_lo[qpar][:, :],
                                start=False, stop=True)
            P.add("tensor", gmm,
                  waits=[(s_wgu[hpar], m_w), (s_xn3[qpar], m_x3), (s_ve, m_xs),
                         (s_sc, m_gsil.get((qi, ht - 2) if ht >= 2 else (qi - 1, HT - 2 + ht), 0))],
                  inc=s_tn)
            m_gmm[(qi, ht)] = s_tn.n
            def umm(e, qpar=qpar, hpar=hpar):
                for c in range(DC):
                    e.matmul(ps_u[hpar][:, :], wu_sb[hpar][:, c * 128:(c + 1) * 128],
                             xn3[qpar][:, c * 512:(c + 1) * 512],
                             start=(c == 0), stop=False)
                e.matmul(ps_u[hpar][:, :], n128_bf[:, :], xs_hi[qpar][:, :],
                         start=False, stop=False)
                return e.matmul(ps_u[hpar][:, :], n128_bf[:, :], xs_lo[qpar][:, :],
                                start=False, stop=True)
            P.add("tensor", umm,
                  waits=[(s_sc, m_uev.get((qi, ht - 2) if ht >= 2 else (qi - 1, HT - 2 + ht), 0))],
                  inc=s_tn)
            m_umm[(qi, ht)] = s_tn.n
            if ht > 0:
                P.add("tensor",
                      lambda e, qi=qi, ht=ht: e.matmul(
                          ps_hs[0:1, :], ones_bf[:, :],
                          hg_sb[:, (ht - 1) * 512:ht * 512],
                          start=(ht - 1 == 0), stop=(ht - 1 == HT - 1)),
                      waits=[(s_ve, m_hg[(qi, ht - 1)]),
                             (s_ve, m_hs.get(qi - 1, 0))],
                      inc=s_tn)
                m_hsmm[(qi, ht - 1)] = s_tn.n
            P.add("scalar",
                  lambda e, ht=ht, hpar=hpar: e.activation(
                      gate_b[hpar][:, :], ps_g[hpar][:, :], ACTF.Silu,
                      scale=sg_sb[:, ht:ht + 1]),
                  waits=[(s_tn, m_gmm[(qi, ht)]),
                         (s_ve, m_hg.get((qi, ht - 2) if ht >= 2 else (qi - 1, HT - 2 + ht), 0))],
                  inc=s_sc)
            m_gsil[(qi, ht)] = s_sc.n
            P.add("scalar",
                  lambda e, ht=ht, hpar=hpar: e.activation(
                      up_b[hpar][:, :], ps_u[hpar][:, :], ACTF.Copy,
                      scale=su_sb[:, ht:ht + 1]),
                  waits=[(s_tn, m_umm[(qi, ht)])],
                  inc=s_sc)
            m_uev[(qi, ht)] = s_sc.n
            P.add("vector",
                  lambda e, ht=ht, hpar=hpar: e.tensor_tensor(
                      hg_sb[:, ht * 512:(ht + 1) * 512], gate_b[hpar][:, :],
                      up_b[hpar][:, :], ALU.mult),
                  waits=[(s_sc, m_uev[(qi, ht)])],
                  inc=s_ve)
            m_hg[(qi, ht)] = s_ve.n
        P.add("tensor",
              lambda e, qi=qi: e.matmul(ps_hs[0:1, :], ones_bf[:, :],
                                        hg_sb[:, (HT - 1) * 512:HT * 512],
                                        start=False, stop=True),
              waits=[(s_ve, m_hg[(qi, HT - 1)])], inc=s_tn)
        m_hsmm[(qi, HT - 1)] = s_tn.n
        P.add("vector", lambda e: e.tensor_copy(hs32[:, :], ps_hs[0:1, :]),
              waits=[(s_tn, m_hsmm[(qi, HT - 1)])], inc=s_ve)
        m_h32 = s_ve.n
        P.add("vector", lambda e, qpar=qpar: e.tensor_copy(hs_hi[qpar][:, :], hs32[:, :]),
              waits=[(s_ve, m_h32)], inc=s_ve)
        m_hhi = s_ve.n
        P.add("vector",
              lambda e, qpar=qpar: e.tensor_tensor(hs_lo[qpar][:, :], hs32[:, :],
                                                   hs_hi[qpar][:, :], ALU.subtract),
              waits=[(s_ve, m_hhi)], inc=s_ve)
        m_hs[qi] = s_ve.n
        for dt in range(DC):
            dpar = dt % 2
            n_wd += 1
            P.add("sync",
                  lambda e, dt=dt, dpar=dpar: e.dma_start(out=wd_sb[dpar][:, :], in_=wd_t[dt, :, :]),
                  waits=[(s_tn, m_dmm.get((qi, dt - 2) if dt >= 2 else (qi - 1, DC - 2 + dt), 0))],
                  inc=s_wd[dpar], k=16)
            m_wdl = s_wd[dpar].n
            P.add("gpsimd",
                  lambda e, qi=qi, dt=dt, dpar=dpar: e.dma_start(
                      out=so_sb[dpar][:, :],
                      in_=sown[dt * 128:(dt + 1) * 128, qi * 512:(qi + 1) * 512]),
                  waits=[(s_cc, 1),
                         (s_ve, m_sg.get((qi, dt - 2) if dt >= 2 else (qi - 1, DC - 2 + dt), 0))],
                  inc=s_so[dpar], k=16)
            m_sol = s_so[dpar].n
            P.add("sync",
                  lambda e, qi=qi, dt=dt, dpar=dpar: e.dma_start(
                      out=xt_sb[dpar][:, :],
                      in_=xT[dt * 128:(dt + 1) * 128, qi * 512:(qi + 1) * 512]),
                  waits=[(s_ve, m_out.get((qi, dt - 2) if dt >= 2 else (qi - 1, DC - 2 + dt), 0))],
                  inc=s_xt[dpar], k=16)
            m_xtl = s_xt[dpar].n
            def dmm(e, qi=qi, dpar=dpar, qpar=qpar):
                for hc in range(HT):
                    e.matmul(ps_d[dpar][:, :], wd_sb[dpar][:, hc * 128:(hc + 1) * 128],
                             hg_sb[:, hc * 512:(hc + 1) * 512],
                             start=(hc == 0), stop=False)
                e.matmul(ps_d[dpar][:, :], n128_bf[:, :], hs_hi[qpar][:, :],
                         start=False, stop=False)
                return e.matmul(ps_d[dpar][:, :], n128_bf[:, :], hs_lo[qpar][:, :],
                                start=False, stop=True)
            P.add("tensor", dmm,
                  waits=[(s_wd[dpar], m_wdl), (s_ve, m_hs[qi]),
                         (s_sc, m_gf.get((qi, dt - 2) if dt >= 2 else (qi - 1, DC - 2 + dt), 0))],
                  inc=s_tn)
            m_dmm[(qi, dt)] = s_tn.n
            P.add("scalar",
                  lambda e, dt=dt, dpar=dpar: e.activation(
                      g_sb[dpar][:, :], ps_d[dpar][:, :], ACTF.Silu,
                      scale=sd_sb[:, dt:dt + 1]),
                  waits=[(s_tn, m_dmm[(qi, dt)]),
                         (s_ve, m_sg.get((qi, dt - 2) if dt >= 2 else (qi - 1, DC - 2 + dt), 0))],
                  inc=s_sc)
            m_gf[(qi, dt)] = s_sc.n
            P.add("vector",
                  lambda e, dpar=dpar: e.tensor_tensor(sgf[dpar][:, :], so_sb[dpar][:, :],
                                                       g_sb[dpar][:, :], ALU.mult),
                  waits=[(s_sc, m_gf[(qi, dt)]), (s_so[dpar], m_sol)],
                  inc=s_ve)
            m_sg[(qi, dt)] = s_ve.n
            n_out += 1
            P.add("vector",
                  lambda e, dpar=dpar: e.tensor_tensor(out_st[dpar][:, :], sgf[dpar][:, :],
                                                       xt_sb[dpar][:, :], ALU.add),
                  waits=[(s_xt[dpar], m_xtl), (s_ve, m_sg[(qi, dt)]),
                         (s_out[dpar], 16 * ((n_out - 1) // 2))],
                  inc=s_ve)
            m_out[(qi, dt)] = s_ve.n
            P.add("gpsimd",
                  lambda g_, qi=qi, dt=dt, dpar=dpar: g_.dma_start(
                      out=y[dt * 128:(dt + 1) * 128, qi * 512:(qi + 1) * 512],
                      in_=out_st[dpar][:, :]),
                  waits=[(s_ve, m_out[(qi, dt)])], inc=s_out[dpar], k=16)

    if DEBUG:
        dbg_w = [(s_kst[0], s_kst[0].n), (s_kst[1], s_kst[1].n),
                 (s_zst[0], s_zst[0].n), (s_zst[1], s_zst[1].n), (s_ve, s_ve.n)]
        P.add("gpsimd", lambda g_: g_.dma_start(out=dbg["d_kd"], in_=kd[:, :]),
              waits=dbg_w, inc=s_out[0], k=16)
        P.add("gpsimd", lambda g_: g_.dma_start(out=dbg["d_qd"], in_=qd[:, :]), inc=s_out[0], k=16)
        P.add("gpsimd", lambda g_: g_.dma_start(out=dbg["d_vd"], in_=vd[:, :]), inc=s_out[0], k=16)
        P.add("gpsimd", lambda g_: g_.dma_start(out=dbg["d_zd"], in_=zd[:, :]), inc=s_out[0], k=16)
        P.add("gpsimd", lambda g_: g_.dma_start(out=dbg["d_xnd"], in_=xnd[:, :, :]), inc=s_out[0], k=16)
        P.add("gpsimd", lambda g_: g_.dma_start(out=dbg["d_xsum"], in_=xsum_sb[:, :]), inc=s_out[0], k=16)
        P.add("gpsimd", lambda g_: g_.dma_start(out=dbg["d_sown"], in_=sown[:, :]),
              waits=[(s_cc, 1)], inc=s_out[0], k=16)

    fin0, fin1 = s_out[0].n, s_out[1].n
    P.add("gpsimd", lambda g_: g_.nop(),
          waits=[(s_out[0], fin0), (s_out[1], fin1)])

    ph3.close()

    with nc.Block() as block:
        P.emit(block)
    ctx.close()
    return nc


_NC_CACHE = None


def make_in_maps(L0_attn_x, L0_attn_W_K, L0_attn_W_Q, L0_attn_W_V, L0_attn_W_O,
                 L0_ffn_wq_gate, L0_ffn_s_gate, L0_ffn_wq_up, L0_ffn_s_up,
                 L0_ffn_wq_down, L0_ffn_s_down):
    xf = np.asarray(L0_attn_x, dtype=np.float32)
    W_K = np.asarray(L0_attn_W_K, dtype=np.float32)
    W_Q = np.asarray(L0_attn_W_Q, dtype=np.float32)
    W_V = np.asarray(L0_attn_W_V, dtype=np.float32)
    W_O = np.asarray(L0_attn_W_O, dtype=np.float32)
    wqg = np.asarray(L0_ffn_wq_gate)
    wqu = np.asarray(L0_ffn_wq_up)
    wqd = np.asarray(L0_ffn_wq_down)
    s_g = np.asarray(L0_ffn_s_gate, dtype=np.float32)
    s_u = np.asarray(L0_ffn_s_up, dtype=np.float32)
    s_d = np.asarray(L0_ffn_s_down, dtype=np.float32)

    wg_t = np.ascontiguousarray(
        wqg.reshape(HT, 128, DC, 128).transpose(0, 3, 2, 1).reshape(HT, 128, DC * 128)
    ).astype(BF)
    wu_t = np.ascontiguousarray(
        wqu.reshape(HT, 128, DC, 128).transpose(0, 3, 2, 1).reshape(HT, 128, DC * 128)
    ).astype(BF)
    wd_t = np.ascontiguousarray(
        wqd.reshape(DC, 128, HT, 128).transpose(0, 3, 2, 1).reshape(DC, 128, HT * 128)
    ).astype(BF)
    sg_in = np.ascontiguousarray(s_g.reshape(HT, 128).T)
    su_in = np.ascontiguousarray(s_u.reshape(HT, 128).T)
    sd_in = np.ascontiguousarray(s_d.reshape(DC, 128).T)
    identm = np.eye(128, dtype=np.float32)
    r = np.arange(128)
    mneg_in = np.where(r[None, :] <= r[:, None], 0.0, MNEG).astype(np.float32)

    in_maps = []
    for c in range(8):
        b, hh = c // 2, c % 2
        heads = list(range(hh * NH, hh * NH + NH))
        xb = np.ascontiguousarray(xf[b])
        x_own = np.ascontiguousarray(xb[hh * TOK:(hh + 1) * TOK, :])
        xTc = np.ascontiguousarray(x_own.T)
        def attn_w(W):
            sel = W[heads]
            t = sel.reshape(NH, 128, DC, 128).transpose(3, 0, 2, 1)
            return np.ascontiguousarray(t.reshape(128, NH * DC * 128))
        woh = W_O.reshape(DC, 128, 16, 128)[:, :, heads, :]
        wo_tc = np.ascontiguousarray(woh.transpose(3, 0, 2, 1).reshape(128, DC * NH * 128))
        in_maps.append({
            "x": xb, "x_own": x_own, "xT": xTc,
            "wq_t": attn_w(W_Q), "wk_t": attn_w(W_K), "wv_t": attn_w(W_V),
            "wo_t": wo_tc,
            "wg_t": wg_t, "wu_t": wu_t, "wd_t": wd_t,
            "sg_in": sg_in, "su_in": su_in, "sd_in": sd_in,
            "ident": identm, "mneg_in": mneg_in,
        })

    return in_maps


def kernel(**inputs):
    global _NC_CACHE
    in_maps = make_in_maps(**inputs)
    if _NC_CACHE is None:
        _NC_CACHE = build_nc()
    res = run_bass_kernel_spmd(_NC_CACHE, in_maps, core_ids=list(range(8)))

    out = np.empty((4, S, D), dtype=np.float32)
    for c in range(8):
        b, hh = c // 2, c % 2
        out[b, hh * TOK:(hh + 1) * TOK, :] = np.asarray(res.results[c]["y"]).T
    if DEBUG:
        return out, res
    return out
